# version 02
# baseline (speedup 1.0000x reference)
import math
import numpy as np
import ml_dtypes
import concourse.bass as bass
import concourse.mybir as mybir
from concourse.bass_utils import run_bass_kernel_spmd

F32 = mybir.dt.float32
BF16 = mybir.dt.bfloat16
I32 = mybir.dt.int32
AF = mybir.ActivationFunctionType
ALU = mybir.AluOpType
AX = mybir.AxisListType
BF16NP = ml_dtypes.bfloat16

NCORE = 8
B, S, D = 2, 4096, 2048
NH, DN, DR, DV, RANK = 16, 128, 64, 128, 512
DQK = DN + DR
HF = 8192
RL = 16
TOK = 1024
CH = 512
NCH = S // CH
DC = D // 128
L0_SCALE = DQK ** -0.5
L1_SCALE = DV ** -0.5
EPS = 1e-5
NEG = -30000.0

ENGS = ("tensor", "vector", "scalar", "gpsimd", "sync")
CSEM = {"tensor": "MM", "vector": "VE", "scalar": "SC", "gpsimd": "GC"}
DSEM = {"sync": "DS", "scalar": "DA", "gpsimd": "DG"}


def _tok_rows(j):
    return list(range(j * TOK, (j + 1) * TOK))


def _rank_of_chunk(g):
    return g // 2, g % 2



def _l0_wq_perm():
    nope, ee, oo = [], [], []
    for h in range(NH):
        base = h * DQK
        nope.extend(range(base, base + DN))
        ee.extend(base + DN + 2 * i for i in range(DR // 2))
        oo.extend(base + DN + 2 * i + 1 for i in range(DR // 2))
    return np.array(nope + ee + oo, dtype=np.int64)


def _l0_wkva_perm():
    ee = [RANK + 2 * i for i in range(DR // 2)]
    oo = [RANK + 2 * i + 1 for i in range(DR // 2)]
    return np.array(list(range(RANK)) + ee + oo, dtype=np.int64)


def _l0_wkvb_perm():
    kn, vv = [], []
    for h in range(NH):
        base = h * (DN + DV)
        kn.extend(range(base, base + DN))
        vv.extend(range(base + DN, base + DN + DV))
    return np.array(kn + vv, dtype=np.int64)


def _l1_qk_perm():
    ee, oo = [], []
    for h in range(NH):
        base = h * DV
        ee.extend(base + 2 * i for i in range(DV // 2))
        oo.extend(base + 2 * i + 1 for i in range(DV // 2))
    return np.array(ee + oo, dtype=np.int64)


def _l1_cos_sin():
    half = DV // 2
    inv = 10000.0 ** (-2.0 * np.arange(half, dtype=np.float64) / DV)
    ang = np.arange(S, dtype=np.float64)[:, None] * inv[None, :]
    return (np.cos(ang).T.astype(np.float32).copy(),
            np.sin(ang).T.astype(np.float32).copy())


def _cshard(a, axis, c, n=NCORE):
    sz = a.shape[axis] // n
    sl = [slice(None)] * a.ndim
    sl[axis] = slice(c * sz, (c + 1) * sz)
    return np.ascontiguousarray(a[tuple(sl)])


def prep_host(inp):
    f = lambda k: np.asarray(inp[k])
    sh = {}
    sh["wq"] = np.ascontiguousarray(f("L0_attn_Wq")[:, _l0_wq_perm()])
    sh["wkva"] = np.ascontiguousarray(f("L0_attn_Wkv_a")[:, _l0_wkva_perm()])
    sh["wkvb"] = np.ascontiguousarray(f("L0_attn_Wkv_b")[:, _l0_wkvb_perm()])
    sh["wout"] = np.ascontiguousarray(f("L0_attn_Wout"))
    sh["fcT"] = np.ascontiguousarray(f("L0_attn_freqs_cis").T)
    sh["gamma"] = np.ascontiguousarray(f("L0_attn_kv_gamma").reshape(4, 128).T)
    sh["beta"] = np.ascontiguousarray(f("L0_attn_kv_beta").reshape(4, 128).T)
    sh["up_la"] = np.ascontiguousarray(f("L0_ffn_w_up_lora_a"))
    sh["up_laP"] = np.zeros((D, 128), np.float32)
    sh["up_laP"][:, :RL] = f("L0_ffn_w_up_lora_a")
    sh["dn_laP"] = np.zeros((HF, 128), np.float32)
    sh["dn_laP"][:, :RL] = f("L0_ffn_w_down_lora_a")
    sh["up_lb"] = np.ascontiguousarray(f("L0_ffn_w_up_lora_b"))
    sh["dn_la"] = np.ascontiguousarray(f("L0_ffn_w_down_lora_a"))
    sh["dn_lb"] = np.ascontiguousarray(f("L0_ffn_w_down_lora_b"))
    sh["b_up"] = np.ascontiguousarray(f("L0_ffn_b_up").reshape(HF // 128, 128).T)
    sh["b_dn"] = np.ascontiguousarray(f("L0_ffn_b_down").reshape(DC, 128).T)
    qkp = _l1_qk_perm()
    sh["l1_wq"] = np.ascontiguousarray(f("L1_attn_wq").T[:, qkp])
    sh["l1_wk"] = np.ascontiguousarray(f("L1_attn_wk").T[:, qkp])
    sh["l1_wv"] = np.ascontiguousarray(f("L1_attn_wv").T)
    sh["l1_wo"] = np.ascontiguousarray(f("L1_attn_wo").T)
    sh["l1_cos"], sh["l1_sin"] = _l1_cos_sin()
    sh["g_s"] = np.ascontiguousarray(f("L1_ffn_gate_scale").reshape(HF // 128, 128).T)
    sh["u_s"] = np.ascontiguousarray(f("L1_ffn_up_scale").reshape(HF // 128, 128).T)
    sh["d_s"] = np.ascontiguousarray(f("L1_ffn_down_scale").reshape(DC, 128).T)

    x = f("L0_attn_hidden_states")
    fcT = sh["fcT"]
    cores = []
    for c in range(NCORE):
        bidx, j = c // 4, c % 4
        rows = _tok_rows(j)
        d = {}
        d["xT"] = np.ascontiguousarray(x[bidx, rows, :].T.astype(np.float32))
        p0 = (j * TOK) % 2048
        d["thT"] = np.ascontiguousarray(np.tile(fcT[:, p0:p0 + TOK], (4, 1)))
        d["l1_cosT"] = np.ascontiguousarray(np.tile(sh["l1_cos"][:, j * TOK:(j + 1) * TOK], (2, 1)))
        d["l1_sinT"] = np.ascontiguousarray(np.tile(sh["l1_sin"][:, j * TOK:(j + 1) * TOK], (2, 1)))
        d["upq"] = _cshard(f("L0_ffn_w_up_q"), 1, c)
        d["ups"] = _cshard(f("L0_ffn_w_up_scale"), 1, c)
        d["dnq"] = _cshard(f("L0_ffn_w_down_q"), 1, c)
        d["dns"] = _cshard(f("L0_ffn_w_down_scale"), 1, c)
        d["gq"] = _cshard(f("L1_ffn_gate_w"), 1, c)
        d["uq"] = _cshard(f("L1_ffn_up_w"), 1, c)
        d["dq"] = _cshard(f("L1_ffn_down_w"), 1, c)
        cores.append(d)
    return sh, cores


WSPEC = {
    "up": (D, HF, "upq", "ups", "blk64", 7.5),
    "dn": (HF, D, "dnq", "dns", "blk64", 7.5),
    "g": (D, HF, "gq", "g_s", "row", 128.0),
    "u": (D, HF, "uq", "u_s", "row", 128.0),
    "dd": (HF, D, "dq", "d_s", "row", 128.0),
}



class Builder:
    def __init__(self, name):
        import os
        self.nc = bass.Bass("TRN2", target_bir_lowering=False, name=name,
                            detect_race_conditions=os.environ.get("KRACE", "0") == "1")
        self.q = {e: [] for e in ENGS}
        self.sems = {}
        self.counts = {}
        self.last_wait = {e: {} for e in ENGS}
        self._regions = []
        for s in ("MM", "VE", "SC", "GC", "DS", "DA", "DG", "CC"):
            self.sem(s)

    def sem(self, name):
        if name not in self.sems:
            self.sems[name] = self.nc.alloc_semaphore(name)
            self.counts[name] = 0
        return self.sems[name]

    def tok(self, name):
        return self.counts[name]

    def sb(self, name, shape, dtype):
        if self._regions:
            g = self.nc.sbuf_tensor(name, shape, dtype)
            h = g.__enter__()
            self._regions[-1].append(g)
            return h
        return self.nc.alloc_sbuf_tensor(name, shape, dtype)

    def push_region(self):
        self._regions.append([])

    def pop_region(self):
        for g in reversed(self._regions.pop()):
            g.__exit__(None, None, None)

    def ps(self, name, shape, dtype=F32):
        return self.nc.alloc_psum_tensor(name, shape, dtype)

    def dram(self, name, shape, dtype, kind="Internal", addr_space="Local"):
        return self.nc.dram_tensor(name, shape, dtype, kind=kind,
                                   addr_space=addr_space).ap()

    def op(self, eng, fn, inc=None, wait=(), dma=False, cc=False):
        waits = []
        for s, v in wait:
            if v > 0 and self.last_wait[eng].get(s, -1) < v:
                waits.append((self.sems[s], v))
                self.last_wait[eng][s] = v
        amt = 16 if dma else 1
        tokv = None
        semh = None
        if inc is not None:
            semh = self.sems[inc]
            if cc:
                amt = 1
            self.counts[inc] += amt
            tokv = self.counts[inc]

        def run(e, fn=fn, waits=waits, semh=semh, amt=amt):
            for s, v in waits:
                e.wait_ge(s, v)
            ins = fn(e)
            if semh is not None:
                ins.then_inc(semh, amt)
        self.q[eng].append(run)
        return tokv

    def dma(self, queue, out, in_, wait=(), sem=None, **kw):
        if sem is None:
            sem = DSEM[queue]
        self.sem(sem)
        return self.op(queue, lambda e: e.dma_start(out=out, in_=in_, **kw),
                       inc=sem, wait=wait, dma=True)

    def barrier(self, engines=ENGS):
        self.sem("BR")
        for e in engines:
            self.op(e, lambda en: en.nop(), inc="BR")
        tot = self.tok("BR")
        for e in engines:
            self.op(e, lambda en: en.nop(), wait=[("BR", tot)])

    def finish(self, final_waits=None):
        if final_waits is None:
            final_waits = [(s, v) for s, v in self.counts.items() if v > 0]
        self.op("scalar", lambda e: e.nop(), wait=final_waits)
        with self.nc.Block() as block:
            for e in ENGS:
                ops = self.q[e]
                getattr(block, e)(lambda eng, ops=ops: [f(eng) for f in ops])
        return self.nc



def emit_p1(b, ins, stg):
    nc = b.nc
    ident = nc.inline_tensor(np.eye(128, dtype=BF16NP), name="ident").ap()
    identt = b.sb("p1_ident", [128, 128], BF16)
    t_id = b.dma("sync", identt[:], ident)

    NB = 4
    qt = [b.sb(f"p1_qt{i}", [128, 1024], I32) for i in range(NB)]
    qf = b.sb("p1_qf", [128, 1024], F32)
    wnat = [b.sb(f"p1_wn{i}", [128, 1024], BF16) for i in range(NB)]
    sct = [b.sb(f"p1_sc{i}", [128, 16], F32) for i in range(NB)]
    srow = {nm: b.sb(f"p1_srow_{nm}", [128, WSPEC[nm][1] // 128], F32)
            for nm in WSPEC if WSPEC[nm][4] == "row"}
    zconst = b.sb("p1_zc", [128, 1], F32)
    strip = [b.sb(f"p1_strip{i}", [128, 512], BF16) for i in range(2)]
    psb = [b.ps(f"p1_ps{i}", [128, 512], BF16) for i in range(2)]

    b.op("vector", lambda e: e.memset(zconst[:], -128.0), inc="VE")
    t_zc = b.tok("VE")
    t_srow = {}
    for nm in srow:
        t_srow[nm] = b.dma("scalar", srow[nm][:], ins[nm + "_s"])

    tiles = []
    for nm, (di_t, do_t, qk, sk, kind, zero) in WSPEC.items():
        di_sl = di_t // NCORE
        for doc in range(do_t // 128):
            tiles.append((nm, doc, di_sl, do_t, kind, zero))

    wn_tok = [0] * NB
    wn_free = [0] * NB
    sc_free = [0] * NB
    strip_tok = [0] * 2
    strip_free = [0] * 2
    ndi_prev = 0

    for i, (nm, doc, di_sl, do_t, kind, zero) in enumerate(tiles):
        par = i % NB
        ndi = di_sl // 128
        t_q = b.dma("gpsimd", qt[par][:, 0:di_sl],
                    ins[nm + "_q"][doc * 128:(doc + 1) * 128, :],
                    wait=[("VE", wn_tok[par])], sem=f"p1q{par}")
        ws = f"p1q{par}"
        if kind == "blk64":
            t_s = b.dma("scalar", sct[par][:, 0:di_sl // 64],
                        ins[nm + "_s"][doc * 128:(doc + 1) * 128, :],
                        wait=[("VE", sc_free[par])], sem=f"p1s{par}")
            ss = f"p1s{par}"
        else:
            t_s, ss = t_srow[nm], "DA"
        b.op("vector", lambda e, par=par, di_sl=di_sl: e.tensor_copy(
            out=qf[:, 0:di_sl], in_=qt[par][:, 0:di_sl]),
            inc="VE", wait=[(ws, t_q), ("MM", wn_free[par])])
        if kind == "blk64":
            nb64 = di_sl // 64
            b.op("vector", lambda e, par=par, di_sl=di_sl, nb64=nb64, zero=zero: e.scalar_tensor_tensor(
                out=wnat[par][:, 0:di_sl].rearrange("p (s k) -> p s k", k=64),
                in0=qf[:, 0:di_sl].rearrange("p (s k) -> p s k", k=64),
                scalar=-zero,
                in1=sct[par][:, 0:nb64].broadcast_to([128, nb64, 64]),
                op0=ALU.add, op1=ALU.mult),
                inc="VE", wait=[(ss, t_s)])
            sc_free[par] = b.tok("VE")
        else:
            b.op("vector", lambda e, par=par, di_sl=di_sl, nm=nm, doc=doc: e.tensor_scalar(
                out=wnat[par][:, 0:di_sl], in0=qf[:, 0:di_sl],
                scalar1=zconst[:], scalar2=srow[nm][:, doc:doc + 1],
                op0=ALU.add, op1=ALU.mult),
                inc="VE", wait=[(ss, t_s), ("VE", t_zc)])
        wn_tok[par] = b.tok("VE")

        if par == NB - 1:
            base = i - 3
            vtok = max(wn_tok)
            for di in range(ndi):
                sp = (i // NB * ndi + di) % 2
                for t in range(4):
                    b.op("tensor",
                         lambda e, t=t, di=di, sp=sp: e.transpose(
                             psb[sp][:, t * 128:(t + 1) * 128],
                             wnat[t][:, di * 128:(di + 1) * 128],
                             identt[:]),
                         inc="MM" if t == 3 else None,
                         wait=([("VE", vtok), ("DS", t_id),
                                ("SC", strip_tok[sp])] if t == 0 else []))
                t_mm = b.tok("MM")
                b.op("scalar", lambda e, sp=sp: e.copy(
                    out=strip[sp][:], in_=psb[sp][:]),
                    inc="SC", wait=[("MM", t_mm), (f"p1w{sp}", strip_free[sp])])
                strip_tok[sp] = b.tok("SC")
                dg = doc // 4
                b.dma("sync",
                      stg[nm][di * 128:(di + 1) * 128, dg * 512:(dg + 1) * 512],
                      strip[sp][:],
                      wait=[("SC", strip_tok[sp])], sem=f"p1w{sp}")
                strip_free[sp] = b.tok(f"p1w{sp}")
            for t in range(4):
                wn_free[t] = b.tok("MM")

    return {"stores": b.tok("DS")}



class Ctx:
    def __init__(self, b):
        nc = b.nc
        self.psA = b.ps("psA", [128, 512])
        self.psB = b.ps("psB", [128, 512])
        self.psC = b.ps("psC", [128, 512])
        self.psD = b.ps("psD", [128, 512])
        self.psE = b.ps("psE", [128, 1024])
        self.psF = b.ps("psF", [128, 1024])
        self.slabs = [self.psA, self.psB, self.psC]
        self.slab_free = [[], [], []]
        self.ones_bf = b.sb("ones_bf", [128, 1], BF16)
        self.ones_f = b.sb("ones_f", [1, 128], F32)
        self.ones_bf_row = b.sb("ones_bf_row", [1, 128], BF16)
        self.eps_t = b.sb("eps_t", [1, 1], F32)
        b.op("vector", lambda e: e.memset(self.ones_bf[:], 1.0), inc="VE")
        b.op("vector", lambda e: e.memset(self.ones_f[:], 1.0), inc="VE")
        b.op("vector", lambda e: e.memset(self.ones_bf_row[:], 1.0), inc="VE")
        b.op("vector", lambda e: e.memset(self.eps_t[:], EPS), inc="VE")
        self.t_const = b.tok("VE")
        self.x16 = [b.sb(f"x16_{c}", [128, TOK], BF16) for c in range(DC)]
        self.x16_tok = [("VE", 0)] * DC
        self.wpool = {"st": [b.sb(f"wpool{i}", [128, 32 * 256], BF16)
                             for i in range(2)],
                      "free": [("VE", 0), ("VE", 0)], "sidx": 0}
        self.scr = [b.sb(f"scrF{i}", [128, TOK], F32) for i in range(4)]

    def slab(self, i):
        return self.slabs[i % 3]


def _strips(dout_specs):
    strips = []
    cur = []
    for off, m, meta in dout_specs:
        if cur and (off + m - cur[0][0] > 256):
            strips.append(cur)
            cur = []
        cur.append((off, m, meta))
    if cur:
        strips.append(cur)
    return strips


def emit_wproj(b, ctx, *, tag, wsrc, din_tot, w_f32, dout_specs, rhs, ntok,
               consume, extras=None, wq_queue=None):
    nqt = ntok // 512
    queue = wq_queue or ("gpsimd" if w_f32 else "sync")
    strips = _strips(dout_specs)
    nch = len(rhs)
    parts = []
    p0 = 0
    while p0 < nch:
        parts.append((p0, min(32, nch - p0)))
        p0 += min(32, nch - p0)
    pool = ctx.wpool
    st = pool["st"]
    st_free = pool["free"]
    jobctr = [0]
    for si, strip in enumerate(strips):
        s_off = strip[0][0]
        width = strip[-1][0] + strip[-1][1] - s_off
        ldtoks = []
        slots = []
        for (pc0, pcn) in parts:
            di0 = sum(r[1] for r in rhs[:pc0])
            di_n = sum(r[1] for r in rhs[pc0:pc0 + pcn])
            slot = pool["sidx"] % 2
            pool["sidx"] += 1
            slots.append(slot)
            t = b.dma(queue,
                      st[slot][:].rearrange("p (c m) -> p c m", m=256)[:, 0:pcn, 0:width],
                      wsrc[di0:di0 + di_n, s_off:s_off + width].rearrange(
                          "(c p) m -> p c m", p=128) if di_n > 128 else
                      wsrc[di0:di0 + di_n, s_off:s_off + width].rearrange(
                          "(c p) m -> p c m", p=di_n),
                      wait=[st_free[slot]], sem=f"wp{slot}{queue[0]}")
            ldtoks.append((f"wp{slot}{queue[0]}", t))
        for off, m, meta in strip:
            for qt in range(nqt):
                slabi = jobctr[0] % 3
                jobctr[0] += 1
                ps = ctx.slabs[slabi]
                waits = list(ctx.slab_free[slabi])
                first = True
                for pi, (pc0, pcn) in enumerate(parts):
                    slot = slots[pi]
                    for cj in range(pcn):
                        ci = pc0 + cj
                        tile, pcount, rtok = rhs[ci]
                        w = list(waits) + [ldtoks[pi], rtok]
                        waits = []
                        lhsT = st[slot][:].rearrange(
                            "p (c m) -> p c m", m=256)[0:pcount, cj, off - s_off:off - s_off + m]
                        last = (pi == len(parts) - 1 and cj == pcn - 1 and not extras)
                        b.op("tensor", lambda e, lhsT=lhsT, tile=tile, pcount=pcount,
                             ps=ps, qt=qt, m=m, first=first, last=last: e.matmul(
                                 ps[0:m, :], lhsT, tile[0:pcount, qt * 512:(qt + 1) * 512],
                                 start=first, stop=last),
                             inc="MM" if last else None, wait=w)
                        first = False
                for xi, (lfn, pcount, rtile, xtok) in enumerate(extras or []):
                    last = xi == len(extras) - 1
                    lhsT = lfn(meta)
                    b.op("tensor", lambda e, lhsT=lhsT, rtile=rtile, pcount=pcount,
                         ps=ps, qt=qt, m=m, last=last: e.matmul(
                             ps[0:m, :], lhsT, rtile[0:pcount, qt * 512:(qt + 1) * 512],
                             start=False, stop=last),
                         inc="MM" if last else None, wait=[xtok])
                mm_tok = b.tok("MM")
                ctx.slab_free[slabi] = consume(meta, qt, ps[0:m, :], mm_tok)
        for pi, slot in enumerate(slots):
            st_free[slot] = ("MM", b.tok("MM"))



def emit_load_x16(b, ctx, x_dram):
    for c in range(DC):
        t = b.dma("gpsimd", ctx.x16[c][:], x_dram[c * 128:(c + 1) * 128, :],
                  sem="xld")
    for c in range(DC):
        ctx.x16_tok[c] = ("xld", t)


def emit_theta_cs(b, ctx, thT, nfreq=32):
    th, w1, w2 = ctx.scr[0], ctx.scr[1], ctx.scr[2]
    cs_c = b.sb("cs_c", [nfreq, TOK], F32)
    cs_s = b.sb("cs_s", [nfreq, TOK], F32)
    nhp = b.sb("nhp", [128, 1], F32)
    ppi = b.sb("ppi", [128, 1], F32)
    b.op("vector", lambda e: e.memset(nhp[:], -np.pi / 2), inc="VE")
    b.op("vector", lambda e: e.memset(ppi[:], np.pi), inc="VE")
    t_c = b.tok("VE")
    t_th = b.dma("sync", th[0:nfreq, :], thT, sem="thld")
    b.op("scalar", lambda e: e.activation(out=w1[0:nfreq, :], in_=th[0:nfreq, :], func=AF.Sin,
                                          scale=0.5, bias=nhp[0:nfreq, :]),
         wait=[("thld", t_th), ("VE", t_c)])
    b.op("scalar", lambda e: e.activation(out=w2[0:nfreq, :], in_=th[0:nfreq, :], func=AF.Sin,
                                          scale=-0.5, bias=ppi[0:nfreq, :]),
         inc="SC")
    t_w = b.tok("SC")
    b.op("vector", lambda e: e.tensor_mul(cs_s[:], w1[0:nfreq, :], w2[0:nfreq, :]),
         wait=[("SC", t_w)])
    b.op("vector", lambda e: e.tensor_scalar(out=cs_s[:], in0=cs_s[:],
                                             scalar1=-2.0, scalar2=None,
                                             op0=ALU.mult))
    b.op("vector", lambda e: e.tensor_mul(cs_c[:], w1[0:nfreq, :], w1[0:nfreq, :]))
    b.op("vector", lambda e: e.tensor_scalar(out=cs_c[:], in0=cs_c[:],
                                             scalar1=2.0, scalar2=-1.0,
                                             op0=ALU.mult, op1=ALU.add),
         inc="VE")
    return cs_c, cs_s, ("VE", b.tok("VE"))


def emit_rope_pair(b, ctx, tag, ein, oin, cs_c, cs_s, cs_tok, nfreq, in_toks,
                   e16, o16, scratch=None):
    if scratch is None:
        m1 = b.sb(f"{tag}_m1", [nfreq, TOK], F32)
        m2 = b.sb(f"{tag}_m2", [nfreq, TOK], F32)
    else:
        m1, m2 = scratch
    b.op("vector", lambda e: e.tensor_mul(m1[0:nfreq, :], ein[:], cs_c[0:nfreq, :]),
         wait=[cs_tok] + list(in_toks))
    b.op("vector", lambda e: e.tensor_mul(m2[0:nfreq, :], oin[:], cs_s[0:nfreq, :]))
    b.op("vector", lambda e: e.tensor_sub(m1[0:nfreq, :], m1[0:nfreq, :], m2[0:nfreq, :]))
    b.op("vector", lambda e: e.tensor_copy(out=e16[:], in_=m1[0:nfreq, :]))
    b.op("vector", lambda e: e.tensor_mul(m1[0:nfreq, :], oin[:], cs_c[0:nfreq, :]))
    b.op("vector", lambda e: e.tensor_mul(m2[0:nfreq, :], ein[:], cs_s[0:nfreq, :]))
    b.op("vector", lambda e: e.tensor_add(m1[0:nfreq, :], m1[0:nfreq, :], m2[0:nfreq, :]))
    b.op("vector", lambda e: e.tensor_copy(out=o16[:], in_=m1[0:nfreq, :]), inc="VE")
    return ("VE", b.tok("VE"))


def emit_p2(b, ctx, *, xT, wkva, wkvb, thT, gamma_d, beta_d, kst, krst, vst):
    import os
    P2STOP = int(os.environ.get("P2STOP", "9"))
    NOSTATS = os.environ.get("P2NOSTATS", "0") == "1"
    NOKE = os.environ.get("P2NOKE", "0") == "1"
    NOTH = os.environ.get("P2NOTH", "0") == "1"
    nc = b.nc
    emit_load_x16(b, ctx, xT)
    if not NOTH:
        cs_c, cs_s, cs_tok = emit_theta_cs(b, ctx, thT, nfreq=128)

    gam = b.sb("gam", [128, 4], F32)
    bet = b.sb("bet", [128, 4], F32)
    t_gb = b.dma("sync", gam[:], gamma_d, sem="gbld")
    t_gb = b.dma("sync", bet[:], beta_d, sem="gbld")

    kvcf = [b.sb(f"kvcf{c}", [128, TOK], F32) for c in range(4)]
    kvn16 = [b.sb(f"kvn16_{c}", [128, TOK], BF16) for c in range(4)]
    ke = b.sb("ke", [32, TOK], F32)
    ko = b.sb("ko", [32, TOK], F32)
    ke16 = b.sb("ke16", [32, TOK], BF16)
    ko16 = b.sb("ko16", [32, TOK], BF16)
    zs = b.sb("zs", [128, 512], BF16)
    sqs = b.sb("sqs", [128, 512], BF16)
    stat_toks = []

    def consume_kv(meta, qt, ps, mm_tok):
        kind = meta[0]
        if kind == "kv":
            c = meta[1]
            b.op("vector", lambda e, c=c, qt=qt, ps=ps: e.tensor_copy(
                out=kvcf[c][:, qt * 512:(qt + 1) * 512], in_=ps),
                inc="VE", wait=[("MM", mm_tok)])
            t_v = b.tok("VE")
            if NOSTATS:
                return [("VE", t_v)]
            src_sl = kvcf[c][:, qt * 512:(qt + 1) * 512]
            b.op("scalar", lambda e, src_sl=src_sl: e.copy(out=zs[:], in_=src_sl),
                 wait=[("VE", t_v), ("MM", stat_clear[0])])
            b.op("scalar", lambda e, src_sl=src_sl: e.activation(
                out=sqs[:], in_=src_sl, func=AF.Square), inc="SC")
            t_s = b.tok("SC")
            b.op("tensor", lambda e, qt=qt, c=c: e.matmul(
                ctx.psF[0:1, qt * 512:(qt + 1) * 512], ctx.ones_bf[:], zs[:],
                start=(c == 0), stop=(c == 3)),
                wait=[("SC", t_s), ("VE", ctx.t_const)])
            b.op("tensor", lambda e, qt=qt, c=c: e.matmul(
                ctx.psE[0:1, qt * 512:(qt + 1) * 512], ctx.ones_bf[:], sqs[:],
                start=(c == 0), stop=(c == 3)),
                inc="MM")
            stat_toks.append(b.tok("MM"))
            stat_clear[0] = b.tok("MM")
            return [("VE", t_v), ("SC", t_s)]
        else:
            dst = ke if kind == "ke" else ko
            b.op("vector", lambda e, dst=dst: e.tensor_copy(
                out=dst[:, qt * 512:(qt + 1) * 512], in_=ps[0:32, :]),
                inc="VE", wait=[("MM", mm_tok)])
            return [("VE", b.tok("VE"))]

    stat_clear = [0]
    specs = [(c * 128, 128, ("kv", c)) for c in range(4)]
    if not NOKE:
        specs += [(512, 32, ("ke",)), (544, 32, ("ko",))]
    emit_wproj(b, ctx, tag="kva", wsrc=wkva, din_tot=D, w_f32=True,
               dout_specs=specs, rhs=[(ctx.x16[c], 128, ctx.x16_tok[c])
                                     for c in range(DC)],
               ntok=TOK, consume=consume_kv)

    if P2STOP <= 1:
        return
    s1 = ctx.scr[0][0:1, :]
    s2 = ctx.scr[1][0:1, :]
    mu = ctx.scr[2][0:1, :]
    rr = ctx.scr[3][0:1, :]
    tstat = stat_toks[-1]
    b.op("vector", lambda e: e.tensor_copy(out=s1, in_=ctx.psF[0:1, :]),
         wait=[("MM", tstat)])
    b.op("vector", lambda e: e.tensor_copy(out=s2, in_=ctx.psE[0:1, :]),
         inc="VE")
    t_ev = b.tok("VE")
    b.op("scalar", lambda e: e.mul(out=mu, in_=s1, mul=1.0 / RANK),
         wait=[("VE", t_ev)])
    b.op("scalar", lambda e: e.mul(out=s2, in_=s2, mul=1.0 / RANK),
         inc="SC")
    t_sc = b.tok("SC")
    b.op("vector", lambda e: e.tensor_mul(s1, mu, mu),
         wait=[("SC", t_sc)])
    b.op("vector", lambda e: e.tensor_sub(s1, s2, s1), inc="VE")
    t_var = b.tok("VE")
    b.op("scalar", lambda e: e.activation(out=s1, in_=s1, func=AF.Sqrt,
                                          bias=ctx.eps_t[:]),
         inc="SC", wait=[("VE", t_var), ("VE", ctx.t_const)])
    t_sq = b.tok("SC")
    b.op("vector", lambda e: e.reciprocal(out=rr, in_=s1),
         inc="VE", wait=[("SC", t_sq)])
    t_r = b.tok("VE")
    btoks = []
    for qt in range(2):
        b.op("tensor", lambda e, qt=qt: e.matmul(
            ctx.psE[:, qt * 512:(qt + 1) * 512], ctx.ones_f[:],
            mu[:, qt * 512:(qt + 1) * 512], start=True, stop=True),
            wait=[("VE", t_r)] + [("VE", t_ev)])
        b.op("tensor", lambda e, qt=qt: e.matmul(
            ctx.psF[:, qt * 512:(qt + 1) * 512], ctx.ones_f[:],
            rr[:, qt * 512:(qt + 1) * 512], start=True, stop=True),
            inc="MM")
        btoks.append(b.tok("MM"))
    tscr = ctx.scr[0][:, 0:512]
    kvn_toks = []
    for c in range(4):
        for qt in range(2):
            sl = slice(qt * 512, (qt + 1) * 512)
            b.op("vector", lambda e, c=c, sl=sl: e.tensor_sub(
                tscr, kvcf[c][:, sl], ctx.psE[:, sl]),
                wait=[("MM", btoks[qt])])
            b.op("vector", lambda e, sl=sl: e.tensor_mul(
                tscr, tscr, ctx.psF[:, sl]))
            b.op("vector", lambda e, c=c, sl=sl: e.tensor_scalar(
                out=kvn16[c][:, sl], in0=tscr,
                scalar1=gam[:, c:c + 1], scalar2=bet[:, c:c + 1],
                op0=ALU.mult, op1=ALU.add),
                inc="VE", wait=[("gbld", t_gb)])
        kvn_toks.append(("VE", b.tok("VE")))

    if P2STOP <= 2:
        return
    ke_tok = ("VE", b.tok("VE"))
    rp_tok = emit_rope_pair(b, ctx, "kr", ke, ko, cs_c, cs_s, cs_tok, 32,
                            [ke_tok], ke16, ko16)
    b.dma("sync", krst[0:32, :], ke16[:], wait=[rp_tok], sem="krw")
    b.dma("sync", krst[32:64, :], ko16[:], wait=[rp_tok], sem="krw")

    if P2STOP <= 3:
        return
    ident = nc.inline_tensor(np.eye(128, dtype=BF16NP), name="ident2").ap()
    identt = b.sb("p2_ident", [128, 128], BF16)
    t_id = b.dma("sync", identt[:], ident, sem="idld")
    knS = [b.sb(f"knS{i}", [128, 512], BF16) for i in range(2)]
    knS_state = [[("VE", 0), 0] for _ in range(2)]
    vtmp = [b.sb(f"vtmp{i}", [128, TOK], BF16) for i in range(2)]
    vtmp_tok = [[0, 0], [0, 0]]
    vtmp_free = [0]
    psD_free = [0]
    vpair = b.sb("vpair", [128, TOK], BF16)
    vp_store = [0]
    kn_i = [0]

    def consume_kvb(meta, qt, ps, mm_tok):
        kind, h = meta
        if kind == "kn":
            slot = kn_i[0] % 2
            kn_i[0] += 1
            b.op("scalar", lambda e, slot=slot: e.copy(out=knS[slot][:], in_=ps),
                 inc="SC", wait=[("MM", mm_tok), (f"knw{slot}", knS_state[slot][1])])
            t = b.tok("SC")
            knS_state[slot][0] = ("SC", t)
            knS_state[slot][1] = b.dma(
                "sync", kst[h * 128:(h + 1) * 128, qt * 512:(qt + 1) * 512],
                knS[slot][:], wait=[("SC", t)], sem=f"knw{slot}")
            return [("SC", t)]
        else:
            hp = h % 2
            b.op("scalar", lambda e, hp=hp, qt=qt, ps=ps: e.copy(
                out=vtmp[hp][:, qt * 512:(qt + 1) * 512], in_=ps),
                inc="SC", wait=[("MM", mm_tok), ("MM", vtmp_free[0])])
            vtmp_tok[hp][qt] = b.tok("SC")
            out = [("SC", vtmp_tok[hp][qt])]
            if hp == 1 and qt == 1:
                pair = h // 2
                psD_bf = ctx.psD[:].bitcast(BF16)
                sctok = max(max(vtmp_tok[0]), max(vtmp_tok[1]))
                for tg in range(2):
                    for t in range(4):
                        for h2 in range(2):
                            b.op("tensor", lambda e, t=t, h2=h2, tg=tg: e.transpose(
                                psD_bf[:, t * 256 + h2 * 128: t * 256 + h2 * 128 + 128],
                                vtmp[h2][:, (tg * 4 + t) * 128:(tg * 4 + t) * 128 + 128],
                                identt[:]),
                                inc="MM" if (t == 3 and h2 == 1) else None,
                                wait=([("SC", sctok), ("idld", t_id),
                                       ("VE", psD_free[0])]
                                      if (t == 0 and h2 == 0) else []))
                    t_tr = b.tok("MM")
                    b.op("vector", lambda e: e.tensor_copy(out=vpair[:], in_=psD_bf[:]),
                         inc="VE", wait=[("MM", t_tr), ("vpw", vp_store[0])])
                    psD_free[0] = b.tok("VE")
                    vp_store[0] = b.dma(
                        "sync",
                        vst.rearrange("(hp t p) d -> hp p t d", hp=8, p=128)[
                            pair, :, tg * 4:(tg + 1) * 4, :],
                        vpair[:].rearrange("p (t d) -> p t d", t=4),
                        wait=[("VE", psD_free[0])], sem="vpw")
                vtmp_free[0] = b.tok("MM")
            return out

    specs2 = [(h * 128, 128, ("kn", h)) for h in range(NH)]
    if P2STOP >= 5:
        specs2 += [((NH + h) * 128, 128, ("v", h)) for h in range(NH)]
    emit_wproj(b, ctx, tag="kvb", wsrc=wkvb, din_tot=RANK, w_f32=True,
               dout_specs=specs2, rhs=[(kvn16[c], 128, kvn_toks[c])
                                      for c in range(4)],
               ntok=TOK, consume=consume_kvb)



def build_p1_program():
    b = Builder("p1")
    nc = b.nc
    ins, stg = {}, {}
    for nm, (di_t, do_t, qk, sk, kind, zero) in WSPEC.items():
        di_sl = di_t // NCORE
        ins[nm + "_q"] = b.dram(nm + "_q", [do_t, di_sl], I32, kind="ExternalInput")
        if kind == "blk64":
            ins[nm + "_s"] = b.dram(nm + "_s", [do_t, di_sl // 64], F32, kind="ExternalInput")
        else:
            ins[nm + "_s"] = b.dram(nm + "_s", [128, do_t // 128], F32, kind="ExternalInput")
        stg[nm] = b.dram(nm + "_t", [di_sl, do_t], BF16, kind="ExternalOutput")
    emit_p1(b, ins, stg)
    b.finish()
    return b, ins, stg



class LnRes:
    def __init__(self, b, ctx, tag, x_old_d, z_d, x_new_d, bias=None, bias_tok=None,
                 x_old_toks=()):
        self.b, self.ctx, self.tag = b, ctx, tag
        self.x_old_d, self.z_d, self.x_new_d = x_old_d, z_d, x_new_d
        self.bias, self.bias_tok = bias, bias_tok
        self.x_old_toks = list(x_old_toks)
        self.xo = [b.sb(f"{tag}_xo{i}", [128, 512], F32) for i in range(2)]
        self.zr = [b.sb(f"{tag}_zr{i}", [128, 512], F32) for i in range(2)]
        self.z16 = b.sb(f"{tag}_z16", [128, 512], BF16)
        self.sq16 = b.sb(f"{tag}_sq16", [128, 512], BF16)
        self.s1 = ctx.scr[0][0:1, :]
        self.s2 = ctx.scr[1][0:1, :]
        self.mu = ctx.scr[2][0:1, :]
        self.rr = ctx.scr[3][0:1, :]
        self.tscr = ctx.scr[0][:, 0:512]
        self.xnr = [b.sb(f"{tag}_xn{i}", [128, 512], F32) for i in range(2)]
        self.xo_free = [("VE", 0), ("VE", 0)]
        self.zr_free = [[], []]
        self.stat_clear = 0
        self.idx = 0
        self.last_stat = 0
        self.xn_tokens = 0

    def consume(self, c, qt, ps, mm_tok):
        b, ctx, tag = self.b, self.ctx, self.tag
        slot = self.idx % 2
        self.idx += 1
        t_xo = b.dma("scalar", self.xo[slot][:],
                     self.x_old_d[c * 128:(c + 1) * 128, qt * 512:(qt + 1) * 512],
                     wait=[self.xo_free[slot]] + self.x_old_toks,
                     sem=f"{tag}xo{slot}")
        if self.bias is None:
            b.op("vector", lambda e, ps=ps, slot=slot: e.tensor_add(
                self.zr[slot][:], ps, self.xo[slot][:]),
                inc="VE",
                wait=[("MM", mm_tok), (f"{tag}xo{slot}", t_xo)] + self.zr_free[slot])
        else:
            bcol = self.bias[:, c:c + 1]
            b.op("vector", lambda e, ps=ps, slot=slot, bcol=bcol: e.scalar_tensor_tensor(
                out=self.zr[slot][:], in0=ps, scalar=bcol, in1=self.xo[slot][:],
                op0=ALU.add, op1=ALU.add),
                inc="VE",
                wait=[("MM", mm_tok), (f"{tag}xo{slot}", t_xo), self.bias_tok]
                + self.zr_free[slot])
        t_z = b.tok("VE")
        self.xo_free[slot] = ("VE", t_z)
        t_zw = b.dma("sync", self.z_d[c * 128:(c + 1) * 128, qt * 512:(qt + 1) * 512],
                     self.zr[slot][:], wait=[("VE", t_z)], sem=f"{tag}zw{slot}")
        b.op("scalar", lambda e, slot=slot: e.copy(out=self.z16[:], in_=self.zr[slot][:]),
             wait=[("VE", t_z), ("MM", self.stat_clear)])
        b.op("scalar", lambda e, slot=slot: e.activation(
            out=self.sq16[:], in_=self.zr[slot][:], func=AF.Square), inc="SC")
        t_s = b.tok("SC")
        b.op("tensor", lambda e, qt=qt, c=c: e.matmul(
            ctx.psF[0:1, qt * 512:(qt + 1) * 512], ctx.ones_bf[:], self.z16[:],
            start=(c == 0), stop=(c == DC - 1)),
            wait=[("SC", t_s), ("VE", ctx.t_const)])
        b.op("tensor", lambda e, qt=qt, c=c: e.matmul(
            ctx.psE[0:1, qt * 512:(qt + 1) * 512], ctx.ones_bf[:], self.sq16[:],
            start=(c == 0), stop=(c == DC - 1)),
            inc="MM")
        self.stat_clear = b.tok("MM")
        self.last_stat = self.stat_clear
        self.zr_free[slot] = [("SC", t_s), (f"{tag}zw{slot}", t_zw)]
        return [("VE", t_z)]

    def finish(self):
        b, ctx, tag = self.b, self.ctx, self.tag
        b.op("vector", lambda e: e.tensor_copy(out=self.s1, in_=ctx.psF[0:1, :]),
             wait=[("MM", self.last_stat)])
        b.op("vector", lambda e: e.tensor_copy(out=self.s2, in_=ctx.psE[0:1, :]),
             inc="VE")
        t_ev = b.tok("VE")
        b.op("scalar", lambda e: e.mul(out=self.mu, in_=self.s1, mul=1.0 / D),
             wait=[("VE", t_ev)])
        b.op("scalar", lambda e: e.mul(out=self.s2, in_=self.s2, mul=1.0 / D),
             inc="SC")
        t_sc = b.tok("SC")
        b.op("vector", lambda e: e.tensor_mul(self.s1, self.mu, self.mu),
             wait=[("SC", t_sc)])
        b.op("vector", lambda e: e.tensor_sub(self.s1, self.s2, self.s1),
             inc="VE")
        t_var = b.tok("VE")
        b.op("scalar", lambda e: e.activation(out=self.s1, in_=self.s1,
                                              func=AF.Sqrt, bias=ctx.eps_t[:]),
             inc="SC", wait=[("VE", t_var), ("VE", ctx.t_const)])
        t_sq = b.tok("SC")
        b.op("vector", lambda e: e.reciprocal(out=self.rr, in_=self.s1),
             inc="VE", wait=[("SC", t_sq)])
        t_r = b.tok("VE")
        btoks = [0, 0]
        for qt in range(2):
            sl = slice(qt * 512, (qt + 1) * 512)
            b.op("tensor", lambda e, sl=sl: e.matmul(
                ctx.psE[:, sl], ctx.ones_f[:], self.mu[:, sl], start=True, stop=True),
                wait=[("VE", t_r)])
            b.op("tensor", lambda e, sl=sl: e.matmul(
                ctx.psF[:, sl], ctx.ones_f[:], self.rr[:, sl], start=True, stop=True),
                inc="MM")
            btoks[qt] = b.tok("MM")
        zw_total = [b.tok(f"{tag}zw{i}") for i in range(2)]
        zr_free = [[], []]
        x16_toks = []
        for c in range(DC):
            for qt in range(2):
                slot = (c * 2 + qt) % 2
                sl = slice(qt * 512, (qt + 1) * 512)
                t_z = b.dma("scalar", self.zr[slot][:],
                            self.z_d[c * 128:(c + 1) * 128, sl],
                            wait=[(f"{tag}zw{i}", zw_total[i]) for i in range(2)]
                            + zr_free[slot], sem=f"{tag}zl{slot}")
                b.op("vector", lambda e, slot=slot, sl=sl: e.tensor_sub(
                    self.tscr, self.zr[slot][:], ctx.psE[:, sl]),
                    wait=[(f"{tag}zl{slot}", t_z), ("MM", btoks[qt])])
                b.op("vector", lambda e, slot=slot, sl=sl: e.tensor_mul(
                    self.xnr[slot][:], self.tscr, ctx.psF[:, sl]),
                    inc="VE")
                t_xn = b.tok("VE")
                t_xw = b.dma("sync", self.x_new_d[c * 128:(c + 1) * 128, sl],
                             self.xnr[slot][:], wait=[("VE", t_xn)],
                             sem=f"{tag}xw{slot}")
                b.op("scalar", lambda e, c=c, sl=sl, slot=slot: e.copy(
                    out=ctx.x16[c][:, sl], in_=self.xnr[slot][:]),
                    inc="SC", wait=[("VE", t_xn)])
                t_16 = b.tok("SC")
                ctx.x16_tok[c] = ("SC", t_16)
                zr_free[slot] = [("VE", t_xn)]
                self.xnr_free = [("SC", t_16), (f"{tag}xw{slot}", t_xw)]
                zr_free[slot] = [("VE", t_xn), ("SC", t_16), (f"{tag}xw{slot}", t_xw)]
        self.xn_done = [(f"{tag}xw{i}", b.tok(f"{tag}xw{i}")) for i in range(2)]



def emit_p4_l0attn(b, ctx, *, wq_d, qrp_d, gk_d, gkr_d, gv_d, cs_c, cs_s,
                   cs_tok, attn16):
    nc = b.nc
    ef, of = ctx.scr[0], ctx.scr[1]
    e16 = b.sb("qe16", [128, TOK], BF16)
    o16 = b.sb("qo16", [128, TOK], BF16)
    rsc = (ctx.scr[2], ctx.scr[3])
    eo_toks = {}
    rope_done = [("VE", 0)]

    def consume_qr(meta, qt, ps, mm_tok):
        kind, c = meta
        dst = ef if kind == "e" else of
        b.op("vector", lambda e, dst=dst, qt=qt, ps=ps: e.tensor_copy(
            out=dst[:, qt * 512:(qt + 1) * 512], in_=ps),
            inc="VE", wait=[("MM", mm_tok), cs_tok] + rope_done)
        eo_toks[(kind, qt)] = b.tok("VE")
        return [("VE", b.tok("VE"))]

    st_tok = [0, 0]
    for c in range(4):
        specs = [(2048 + c * 128, 128, ("e", c)), (2560 + c * 128, 128, ("o", c))]
        emit_wproj(b, ctx, tag="qr", wsrc=wq_d, din_tot=D, w_f32=True,
                   dout_specs=specs, rhs=[(ctx.x16[cc], 128, ctx.x16_tok[cc])
                                         for cc in range(DC)],
                   ntok=TOK, consume=consume_qr)
        in_toks = [("VE", max(eo_toks[("e", 1)], eo_toks[("o", 1)])),
                   ("qrw0", st_tok[0]), ("qrw1", st_tok[1])]
        rp = emit_rope_pair(b, ctx, f"qrope{c}", ef, of, cs_c, cs_s,
                            cs_tok, 128, in_toks, e16, o16, scratch=rsc)
        rope_done = [rp]
        for i in range(4):
            h = 4 * c + i
            st_tok[0] = b.dma("sync", qrp_d[h * 64:h * 64 + 32, :],
                              e16[32 * i:32 * i + 32, :], wait=[rp], sem="qrw0")
            st_tok[1] = b.dma("sync", qrp_d[h * 64 + 32:h * 64 + 64, :],
                              o16[32 * i:32 * i + 32, :], wait=[rp], sem="qrw1")
    qrp_done = [("qrw0", st_tok[0]), ("qrw1", st_tok[1])]

    kr_all = b.sb("kr_all", [64, S], BF16)
    t_kr = b.dma("sync", kr_all[:].rearrange("p (r t) -> p r t", r=4),
                 gkr_d.rearrange("(r p) t -> p r t", p=64), sem="krld")
    krsq = b.sb("krsq", [64, 512], BF16)
    krm = b.sb("krm", [1, 1], F32)
    mscr = b.sb("mscr", [1, 4], F32)
    psD_w = ("VE", 0)
    t_prev_mm = 0
    for s in range(8):
        b.op("scalar", lambda e, s=s: e.activation(
            out=krsq[:], in_=kr_all[:, s * 512:(s + 1) * 512], func=AF.Square),
            inc="SC", wait=[("krld", t_kr), ("MM", t_prev_mm), psD_w])
        t_s = b.tok("SC")
        b.op("tensor", lambda e: e.matmul(
            ctx.psD[0:1, 0:512], ctx.ones_bf[0:64, :], krsq[:],
            start=True, stop=True), inc="MM",
            wait=[("SC", t_s), ("VE", ctx.t_const), psD_w])
        t_prev_mm = b.tok("MM")
        b.op("vector", lambda e: e.reduce_max(
            out=mscr[:, 0:1], in_=ctx.psD[0:1, 0:512], axis=AX.X),
            wait=[("MM", t_prev_mm)])
        if s == 0:
            b.op("vector", lambda e: e.tensor_copy(out=krm[:], in_=mscr[:, 0:1]),
                 inc="VE")
        else:
            b.op("vector", lambda e: e.tensor_max(krm[:], krm[:], mscr[:, 0:1]),
                 inc="VE")
        psD_w = ("VE", b.tok("VE"))
    t_krm = b.tok("VE")

    kh = [b.sb(f"kh{i}", [128, S], BF16) for i in range(2)]
    vp = [b.sb(f"vp{i}", [128, 16 * 256], BF16) for i in range(2)]
    q16 = [b.sb(f"q16_{i}", [128, TOK], BF16) for i in range(2)]
    qr16 = [b.sb(f"qr16_{i}", [64, TOK], BF16) for i in range(2)]
    qsq = b.sb("qsq", [128, 512], BF16)
    qrsq = b.sb("qrsq", [64, 512], BF16)
    ksq = b.sb("ksq", [128, 512], BF16)
    kkm = b.sb("kkm", [1, 1], F32)
    qm = b.sb("qm", [1, 1], F32)
    mb = b.sb("mb", [128, 1], F32)
    rs = ctx.scr[0][0:1, :]
    rb = b.sb("rb", [128, 512], F32)
    pr = [b.sb(f"pr{i}", [128, 512], BF16) for i in range(3)]
    kh_free = [[], []]
    vp_free = [[], []]
    q_free = [[], []]
    qr_free = [[], []]
    pr_free = [[], [], []]
    psE_free = []
    psF_free = []
    attn_toks = [None] * NH
    vp_tok = None

    for h in range(NH):
        par = h % 2
        pairi = h // 2
        if par == 0:
            vp_tok = [None, None]
            for half in range(2):
                for r in (2 * half, 2 * half + 1):
                    t_vp = b.dma("sync",
                                 vp[half][:].rearrange("p (k d) -> p k d", d=256)[
                                     :, (r % 2) * 8:(r % 2) * 8 + 8, :],
                                 gv_d.rearrange("(r hp t p) d -> r hp p t d",
                                                r=4, hp=8, p=128)[r, pairi, :, :, :],
                                 wait=vp_free[half], sem=f"vpl{half}")
                vp_tok[half] = (f"vpl{half}", t_vp)
        t_kh = b.dma("sync", kh[par][:].rearrange("p (r t) -> p r t", r=4),
                     gk_d.rearrange("(r c p) t -> p c r t", c=NH, p=128)[:, h, :, :],
                     wait=kh_free[par], sem=f"khl{par}")
        kh_tok = (f"khl{par}", t_kh)
        t_qr = b.dma("sync", qr16[par][:], qrp_d[h * 64:(h + 1) * 64, :],
                     wait=qr_free[par] + qrp_done, sem=f"qrl{par}")
        qr_tok = (f"qrl{par}", t_qr)

        def consume_q(meta, qt, ps, mm_tok, par=par):
            b.op("scalar", lambda e, qt=qt, ps=ps: e.copy(
                out=q16[par][:, qt * 512:(qt + 1) * 512], in_=ps),
                inc="SC", wait=[("MM", mm_tok)] + q_free[par])
            return [("SC", b.tok("SC"))]
        emit_wproj(b, ctx, tag=f"qp{par}", wsrc=wq_d, din_tot=D, w_f32=True,
                   dout_specs=[(h * 128, 128, 0)],
                   rhs=[(ctx.x16[c], 128, ctx.x16_tok[c]) for c in range(DC)],
                   ntok=TOK, consume=consume_q)
        t_q16 = b.tok("SC")

        qq_mm = 0
        for qt in range(2):
            sl = slice(qt * 512, (qt + 1) * 512)
            b.op("scalar", lambda e, sl=sl, par=par: e.activation(
                out=qsq[:], in_=q16[par][:, sl], func=AF.Square),
                wait=[("SC", t_q16), ("MM", qq_mm)])
            b.op("scalar", lambda e, sl=sl, par=par: e.activation(
                out=qrsq[:], in_=qr16[par][:, sl], func=AF.Square),
                inc="SC", wait=[qr_tok])
            t_qs = b.tok("SC")
            b.op("tensor", lambda e, sl=sl: e.matmul(
                ctx.psF[0:1, sl], ctx.ones_bf[:], qsq[:], start=True, stop=False),
                wait=[("SC", t_qs)] + (psF_free if qt == 0 else []))
            b.op("tensor", lambda e, sl=sl: e.matmul(
                ctx.psF[0:1, sl], ctx.ones_bf[0:64, :], qrsq[:],
                start=False, stop=True), inc="MM")
            qq_mm = b.tok("MM")
        b.op("vector", lambda e: e.reduce_max(out=qm[:], in_=ctx.psF[0:1, 0:1024],
                                              axis=AX.X),
             inc="VE", wait=[("MM", qq_mm)])
        psF_free = [("VE", b.tok("VE"))]

        t_prev = 0
        t_prev_red = 0
        for s in range(8):
            b.op("scalar", lambda e, s=s, par=par: e.activation(
                out=ksq[:], in_=kh[par][:, s * 512:(s + 1) * 512], func=AF.Square),
                inc="SC",
                wait=[kh_tok, ("MM", t_prev), ("VE", t_prev_red)]
                + (psE_free if s == 0 else []))
            t_ks = b.tok("SC")
            b.op("tensor", lambda e: e.matmul(
                ctx.psE[0:1, 0:512], ctx.ones_bf[:], ksq[:],
                start=True, stop=True), inc="MM", wait=[("SC", t_ks)])
            t_prev = b.tok("MM")
            b.op("vector", lambda e: e.reduce_max(
                out=mscr[:, 1:2], in_=ctx.psE[0:1, 0:512], axis=AX.X),
                wait=[("MM", t_prev), ("VE", t_krm)])
            if s == 0:
                b.op("vector", lambda e: e.tensor_copy(out=kkm[:], in_=mscr[:, 1:2]),
                     inc="VE")
            else:
                b.op("vector", lambda e: e.tensor_max(kkm[:], kkm[:], mscr[:, 1:2]),
                     inc="VE")
            t_prev_red = b.tok("VE")
        psE_row_ve = b.tok("VE")
        b.op("vector", lambda e: e.tensor_add(kkm[:], kkm[:], krm[:]))
        b.op("vector", lambda e: e.tensor_mul(kkm[:], kkm[:], qm[:]), inc="VE")
        t_m1 = b.tok("VE")
        b.op("scalar", lambda e: e.activation(out=kkm[:], in_=kkm[:], func=AF.Sqrt),
             wait=[("VE", t_m1)])
        b.op("scalar", lambda e: e.mul(out=kkm[:], in_=kkm[:], mul=-L0_SCALE),
             inc="SC")
        t_m2 = b.tok("SC")
        b.op("tensor", lambda e: e.matmul(ctx.psD[:, 0:1], ctx.ones_f[:], kkm[:],
                                          start=True, stop=True),
             inc="MM", wait=[("SC", t_m2), psD_w])
        t_m3 = b.tok("MM")
        b.op("vector", lambda e: e.tensor_copy(out=mb[:], in_=ctx.psD[:, 0:1]),
             inc="VE", wait=[("MM", t_m3)])
        t_mb = b.tok("VE")
        psD_w = ("VE", t_mb)

        for qt in range(2):
            sl = slice(qt * 512, (qt + 1) * 512)
            for kt in range(32):
                si = (qt * 32 + kt) % 3
                slab = ctx.slabs[si]
                ksl = slice(kt * 128, (kt + 1) * 128)
                b.op("tensor", lambda e, slab=slab, ksl=ksl, sl=sl, par=par: e.matmul(
                    slab[:], kh[par][:, ksl], q16[par][:, sl],
                    start=True, stop=False),
                    wait=list(ctx.slab_free[si]) + [("SC", t_q16), kh_tok]
                    + ([("VE", psE_row_ve)] if kt == 0 and qt == 0 else []))
                b.op("tensor", lambda e, slab=slab, ksl=ksl, sl=sl, par=par: e.matmul(
                    slab[:], kr_all[:, ksl], qr16[par][:, sl],
                    start=False, stop=True),
                    inc="MM", wait=[qr_tok, ("krld", t_kr)])
                t_sc_mm = b.tok("MM")
                b.op("scalar", lambda e, si=si, slab=slab: e.activation(
                    out=pr[si][:], in_=slab[:], func=AF.Exp, scale=L0_SCALE,
                    bias=mb[:]),
                    inc="SC",
                    wait=[("MM", t_sc_mm), ("VE", t_mb)] + pr_free[si])
                t_exp = b.tok("SC")
                ctx.slab_free[si] = [("SC", t_exp)]
                b.op("tensor", lambda e, si=si, qt=qt, kt=kt: e.matmul(
                    ctx.psF[0:1, qt * 512:(qt + 1) * 512], ctx.ones_bf[:],
                    pr[si][:], start=(kt == 0), stop=(kt == 31)),
                    wait=[("SC", t_exp)]
                    + (psF_free if kt == 0 and qt == 0 else []))
                half = kt // 16
                b.op("tensor", lambda e, si=si, qt=qt, kt=kt, half=half, par=par: e.matmul(
                    ctx.psE[:, qt * 512:(qt + 1) * 512],
                    vp[half][:, (kt % 16) * 256 + par * 128: (kt % 16) * 256 + par * 128 + 128],
                    pr[si][:], start=(kt == 0), stop=(kt == 31)),
                    inc="MM",
                    wait=([vp_tok[half]] if kt % 16 == 0 and qt == 0 else []))
                pr_free[si] = [("MM", b.tok("MM"))]
        sweep_mm = b.tok("MM")

        b.op("vector", lambda e: e.reciprocal(out=rs, in_=ctx.psF[0:1, :]),
             inc="VE", wait=[("MM", sweep_mm)])
        t_rs = b.tok("VE")
        psF_free = [("VE", t_rs)]
        nrm = []
        for qt in range(2):
            sl = slice(qt * 512, (qt + 1) * 512)
            b.op("tensor", lambda e: e.matmul(
                ctx.psD[:, 0:512], ctx.ones_f[:], rs[0:1, sl],
                start=True, stop=True),
                inc="MM", wait=[("VE", t_rs), psD_w])
            t_bc = b.tok("MM")
            b.op("vector", lambda e: e.tensor_copy(out=rb[:], in_=ctx.psD[:, 0:512]),
                 inc="VE", wait=[("MM", t_bc)])
            b.op("vector", lambda e, h=h, sl=sl: e.tensor_mul(
                attn16[h][:, sl], ctx.psE[:, sl], rb[:]),
                inc="VE")
            t_n = b.tok("VE")
            psD_w = ("VE", t_n)
            nrm.append(("VE", t_n))
        psE_free = list(nrm)
        attn_toks[h] = list(nrm)
        kh_free[par] = [("MM", sweep_mm)]
        q_free[par] = [("MM", sweep_mm)]
        qr_free[par] = [("MM", sweep_mm)]
        if par == 1:
            vp_free[0] = [("MM", sweep_mm)]
            vp_free[1] = [("MM", sweep_mm)]
    return attn_toks


def build_p2_program():
    b = Builder("p2")
    ctx = Ctx(b)
    xT = b.dram("xT", [D, TOK], F32, kind="ExternalInput")
    wkva = b.dram("wkva", [D, RANK + DR], F32, kind="ExternalInput")
    wkvb = b.dram("wkvb", [RANK, NH * (DN + DV)], F32, kind="ExternalInput")
    thT = b.dram("thT", [128, TOK], F32, kind="ExternalInput")
    gamma_d = b.dram("gamma", [128, 4], F32, kind="ExternalInput")
    beta_d = b.dram("beta", [128, 4], F32, kind="ExternalInput")
    kst = b.dram("kst", [NH * DN, TOK], BF16, kind="ExternalOutput")
    krst = b.dram("krst", [DR, TOK], BF16, kind="ExternalOutput")
    vst = b.dram("vst", [NH // 2 * TOK, 2 * DV], BF16, kind="ExternalOutput")
    emit_p2(b, ctx, xT=xT, wkva=wkva, wkvb=wkvb, thT=thT, gamma_d=gamma_d,
            beta_d=beta_d, kst=kst, krst=krst, vst=vst)
    b.finish()
    return b


def emit_ffn0(b, ctx, *, gup_d, gdn_d, up_laP_d, up_lb_d, dn_laP_d, dn_lb_d,
              b_up_d, b_dn_d, ln):
    NHC = HF // 128
    b_up = b.sb("b_up_t", [128, NHC], F32)
    t_bu = b.dma("sync", b_up[:], b_up_d, sem="bupld")
    up_lb = b.sb("up_lb_t", [16, HF], BF16)
    t_ulb = b.dma("gpsimd", up_lb[:], up_lb_d, sem="ulbld")
    dn_lb = b.sb("dn_lb_t", [16, D], BF16)
    t_dlb = b.dma("gpsimd", dn_lb[:], dn_lb_d, sem="dlbld")
    tu16 = b.sb("tu16", [16, TOK], BF16)
    td16 = b.sb("td16", [16, 512], BF16)
    h16 = [b.sb(f"h16_{o}", [128, 512], BF16) for o in range(NHC)]
    h_tok = [("VE", 0)] * NHC
    td_tok = [("VE", 0)]

    def consume_tu(meta, qt, ps, mm_tok):
        b.op("scalar", lambda e, qt=qt, ps=ps: e.copy(
            out=tu16[:, qt * 512:(qt + 1) * 512], in_=ps[0:16, :]),
            inc="SC", wait=[("MM", mm_tok)])
        return [("SC", b.tok("SC"))]
    emit_wproj(b, ctx, tag="tu", wsrc=up_laP_d, din_tot=D, w_f32=True,
               dout_specs=[(0, 128, 0)],
               rhs=[(ctx.x16[c], 128, ctx.x16_tok[c]) for c in range(DC)],
               ntok=TOK, consume=consume_tu)
    t_tu = b.tok("SC")

    for th in range(2):
        tsl = slice(th * 512, (th + 1) * 512)
        def consume_up(meta, qt, ps, mm_tok, th=th):
            o = meta
            b.op("vector", lambda e, o=o, ps=ps: e.tensor_scalar(
                out=h16[o][:], in0=ps, scalar1=b_up[:, o:o + 1], scalar2=0.0,
                op0=ALU.add, op1=ALU.max),
                inc="VE", wait=[("MM", mm_tok), ("bupld", t_bu), h_tok[o]])
            h_tok[o] = ("VE", b.tok("VE"))
            return [h_tok[o]]
        emit_wproj(b, ctx, tag="up", wsrc=gup_d, din_tot=D, w_f32=False,
                   dout_specs=[(o * 128, 128, o) for o in range(NHC)],
                   rhs=[(ctx.x16[c][:, tsl], 128, ctx.x16_tok[c]) for c in range(DC)],
                   ntok=512, consume=consume_up,
                   extras=[(lambda meta: up_lb[:, meta * 128:(meta + 1) * 128], 16,
                            tu16[:, tsl], ("ulbld", t_ulb))])
        def consume_td(meta, qt, ps, mm_tok):
            b.op("scalar", lambda e, ps=ps: e.copy(out=td16[:], in_=ps[0:16, :]),
                 inc="SC", wait=[("MM", mm_tok), td_tok[0]])
            return [("SC", b.tok("SC"))]
        emit_wproj(b, ctx, tag="td", wsrc=dn_laP_d, din_tot=HF, w_f32=True,
                   dout_specs=[(0, 128, 0)],
                   rhs=[(h16[o], 128, h_tok[o]) for o in range(NHC)],
                   ntok=512, consume=consume_td)
        t_td = b.tok("SC")
        def consume_dn(meta, qt, ps, mm_tok, th=th):
            return ln.consume(meta, th, ps, mm_tok)
        emit_wproj(b, ctx, tag="dn", wsrc=gdn_d, din_tot=HF, w_f32=False,
                   dout_specs=[(c * 128, 128, c) for c in range(DC)],
                   rhs=[(h16[o], 128, h_tok[o]) for o in range(NHC)],
                   ntok=512, consume=consume_dn,
                   extras=[(lambda meta: dn_lb[:, meta * 128:(meta + 1) * 128], 16,
                            td16, ("SC", t_td))])
        td_tok[0] = ("MM", b.tok("MM"))
        for o in range(NHC):
            h_tok[o] = ("MM", b.tok("MM"))
    ln.finish()


def build_p45_program():
    b = Builder("p45")
    ctx = Ctx(b)
    xT = b.dram("xT", [D, TOK], F32, kind="ExternalInput")
    wq = b.dram("wq", [D, NH * DQK], F32, kind="ExternalInput")
    wout = b.dram("wout", [D, D], F32, kind="ExternalInput")
    thT = b.dram("thT", [128, TOK], F32, kind="ExternalInput")
    gk = b.dram("gk", [4 * NH * DN, TOK], BF16, kind="ExternalInput")
    gkr = b.dram("gkr", [4 * DR, TOK], BF16, kind="ExternalInput")
    gv = b.dram("gv", [4 * 8 * TOK, 2 * DV], BF16, kind="ExternalInput")
    x1 = b.dram("x1", [D, TOK], F32, kind="ExternalOutput")
    qrp = b.dram("qrp", [NH * 64, TOK], BF16)
    z_d = b.dram("z0", [D, TOK], F32)
    attn16 = [b.sb(f"attn16_{h}", [128, TOK], BF16) for h in range(NH)]

    emit_load_x16(b, ctx, xT)
    cs_c, cs_s, cs_tok = emit_theta_cs(b, ctx, thT, nfreq=128)
    attn_toks = emit_p4_l0attn(b, ctx, wq_d=wq, qrp_d=qrp, gk_d=gk, gkr_d=gkr,
                               gv_d=gv, cs_c=cs_c, cs_s=cs_s, cs_tok=cs_tok,
                               attn16=attn16)
    ln = LnRes(b, ctx, "ln0", xT, z_d, x1)
    emit_wproj(b, ctx, tag="wo", wsrc=wout, din_tot=D, w_f32=True,
               dout_specs=[(c * 128, 128, c) for c in range(DC)],
               rhs=[(attn16[h], 128, attn_toks[h][-1]) for h in range(NH)],
               ntok=TOK, consume=lambda meta, qt, ps, tok: ln.consume(meta, qt, ps, tok))
    ln.finish()
    b.finish()
    return b



def emit_p7_l1kv(b, ctx, *, wq1_d, wk1_d, wv1_d, csc_d, css_d, qst_d, kst_d,
                 vst_d):
    nc = b.nc
    cs1c = b.sb("cs1c", [128, TOK], F32)
    cs1s = b.sb("cs1s", [128, TOK], F32)
    t_cs = b.dma("sync", cs1c[:], csc_d, sem="cs1ld")
    t_cs = b.dma("sync", cs1s[:], css_d, sem="cs1ld")
    cs_tok = ("cs1ld", t_cs)
    e16 = b.sb("l1e16", [128, TOK], BF16)
    o16 = b.sb("l1o16", [128, TOK], BF16)
    rsc = (ctx.scr[2], ctx.scr[3])
    ef, of = ctx.scr[0], ctx.scr[1]

    for wname, wsrc, st_d in (("q1", wq1_d, qst_d), ("k1", wk1_d, kst_d)):
        eo_toks = {}
        rope_done = [("VE", 0)]
        st_tok = [0, 0]

        def consume_qk(meta, qt, ps, mm_tok):
            kind, c = meta
            dst = ef if kind == "e" else of
            b.op("vector", lambda e, dst=dst, qt=qt, ps=ps: e.tensor_copy(
                out=dst[:, qt * 512:(qt + 1) * 512], in_=ps),
                inc="VE", wait=[("MM", mm_tok), cs_tok] + rope_done)
            eo_toks[(kind, qt)] = b.tok("VE")
            return [("VE", b.tok("VE"))]

        for c in range(8):
            specs = [(c * 128, 128, ("e", c)), (1024 + c * 128, 128, ("o", c))]
            emit_wproj(b, ctx, tag=wname, wsrc=wsrc, din_tot=D, w_f32=True,
                       dout_specs=specs,
                       rhs=[(ctx.x16[cc], 128, ctx.x16_tok[cc]) for cc in range(DC)],
                       ntok=TOK, consume=consume_qk)
            in_toks = [("VE", max(eo_toks[("e", 1)], eo_toks[("o", 1)])),
                       (f"{wname}w0", st_tok[0]), (f"{wname}w1", st_tok[1])]
            rp = emit_rope_pair(b, ctx, f"{wname}rp{c}", ef, of, cs1c, cs1s,
                                cs_tok, 128, in_toks, e16, o16, scratch=rsc)
            rope_done = [rp]
            for i in range(2):
                h = 2 * c + i
                g, idx = h // 4, h % 4
                st_tok[0] = b.dma("sync",
                                  st_d[g * 512 + idx * 128: g * 512 + idx * 128 + 64, :],
                                  e16[64 * i:64 * i + 64, :], wait=[rp],
                                  sem=f"{wname}w0")
                st_tok[1] = b.dma("sync",
                                  st_d[g * 512 + idx * 128 + 64: g * 512 + idx * 128 + 128, :],
                                  o16[64 * i:64 * i + 64, :], wait=[rp],
                                  sem=f"{wname}w1")

    ident = nc.inline_tensor(np.eye(128, dtype=BF16NP), name="ident_p7").ap()
    identt = b.sb("p7_ident", [128, 128], BF16)
    t_id = b.dma("sync", identt[:], ident, sem="idld7")
    vtmp = [b.sb(f"p7vt{i}", [128, TOK], BF16) for i in range(2)]
    vtmp_tok = [[0, 0], [0, 0]]
    vtmp_free = [0]
    psD_free = [0]
    vpair = b.sb("p7vp", [128, TOK], BF16)
    vp_store = [0]

    def consume_v(meta, qt, ps, mm_tok):
        h = meta
        hp2 = h % 2
        b.op("scalar", lambda e, hp2=hp2, qt=qt, ps=ps: e.copy(
            out=vtmp[hp2][:, qt * 512:(qt + 1) * 512], in_=ps),
            inc="SC", wait=[("MM", mm_tok), ("MM", vtmp_free[0])])
        vtmp_tok[hp2][qt] = b.tok("SC")
        out = [("SC", vtmp_tok[hp2][qt])]
        if hp2 == 1 and qt == 1:
            pair = h // 2
            g, lp = pair // 2, pair % 2
            psD_bf = ctx.psD[:].bitcast(BF16)
            sctok = max(max(vtmp_tok[0]), max(vtmp_tok[1]))
            for tg in range(2):
                for t in range(4):
                    for h2 in range(2):
                        b.op("tensor", lambda e, t=t, h2=h2, tg=tg: e.transpose(
                            psD_bf[:, t * 256 + h2 * 128: t * 256 + h2 * 128 + 128],
                            vtmp[h2][:, (tg * 4 + t) * 128:(tg * 4 + t) * 128 + 128],
                            identt[:]),
                            inc="MM" if (t == 3 and h2 == 1) else None,
                            wait=([("SC", sctok), ("idld7", t_id),
                                   ("VE", psD_free[0])]
                                  if (t == 0 and h2 == 0) else []))
                t_tr = b.tok("MM")
                b.op("vector", lambda e: e.tensor_copy(out=vpair[:], in_=psD_bf[:]),
                     inc="VE", wait=[("MM", t_tr), ("vpw7", vp_store[0])])
                psD_free[0] = b.tok("VE")
                vp_store[0] = b.dma(
                    "sync",
                    vst_d.rearrange("(g lp t p) d -> g lp p t d", g=4, lp=2, p=128)[
                        g, lp, :, tg * 4:(tg + 1) * 4, :],
                    vpair[:].rearrange("p (t d) -> p t d", t=4),
                    wait=[("VE", psD_free[0])], sem="vpw7")
            vtmp_free[0] = b.tok("MM")
        return out

    emit_wproj(b, ctx, tag="v1", wsrc=wv1_d, din_tot=D, w_f32=True,
               dout_specs=[(h * 128, 128, h) for h in range(NH)],
               rhs=[(ctx.x16[c], 128, ctx.x16_tok[c]) for c in range(DC)],
               ntok=TOK, consume=consume_v)



def emit_p8_l1attn(b, ctx, *, gq_d, gk_d, gv_d, ao_d):
    nc = b.nc
    masks_np = np.zeros((4, 128, 512), np.float32)
    kk_ = np.arange(128)[:, None]
    qq_ = np.arange(512)[None, :]
    for dd in range(4):
        masks_np[dd] = np.where(dd * 128 + kk_ <= qq_, 0.0, NEG)
    mask_dram = nc.inline_tensor(masks_np.reshape(4 * 128, 512), name="l1masks").ap()
    maskt = [b.sb(f"maskt{i}", [128, 512], F32) for i in range(4)]
    t_mask = 0
    for i in range(4):
        t_mask = b.dma("sync", maskt[i][:], mask_dram[i * 128:(i + 1) * 128, :],
                       sem="maskld")
    mask_tok = ("maskld", t_mask)

    kh = [b.sb(f"l1kh{i}", [128, S], BF16) for i in range(2)]
    qh = [b.sb(f"l1qh{i}", [128, S], BF16) for i in range(2)]
    vp = [b.sb(f"l1vp{i}", [128, 16 * 256], BF16) for i in range(2)]
    qsq = b.sb("l1qsq", [128, 512], BF16)
    ksq = b.sb("l1ksq", [128, 512], BF16)
    kkm = b.sb("l1kkm", [1, 1], F32)
    qm = b.sb("l1qm", [1, 1], F32)
    mscr = b.sb("l1mscr", [1, 4], F32)
    mb = b.sb("l1mb", [128, 1], F32)
    rs = ctx.scr[0][0:1, :]
    rb = b.sb("l1rb", [128, 512], F32)
    pr = [b.sb(f"l1pr{i}", [128, 512], BF16) for i in range(3)]
    ao16 = b.sb("ao16", [128, 512], BF16)
    kh_free = [[], []]
    qh_free = [[], []]
    vp_free = [[], []]
    pr_free = [[], [], []]
    psD_w = ("VE", 0)
    psE_free = []
    psF_free = []
    ao_w = [("VE", 0)]
    ao_st_tok = [0]

    for lh in range(4):
        par = lh % 2
        lp = lh // 2
        if par == 0:
            vp_tok = [None, None]
            for half in range(2):
                for r in (2 * half, 2 * half + 1):
                    t_vp = b.dma("sync",
                                 vp[half][:].rearrange("p (k d) -> p k d", d=256)[
                                     :, (r % 2) * 8:(r % 2) * 8 + 8, :],
                                 gv_d.rearrange("(r lp t p) d -> r lp p t d",
                                                r=4, lp=2, p=128)[r, lp, :, :, :],
                                 wait=vp_free[half], sem=f"l1vp{half}")
                vp_tok[half] = (f"l1vp{half}", t_vp)
        t_kh = b.dma("sync", kh[par][:].rearrange("p (r t) -> p r t", r=4),
                     gk_d.rearrange("(r c p) t -> p c r t", c=4, p=128)[:, lh, :, :],
                     wait=kh_free[par], sem=f"l1khl{par}")
        kh_tok = (f"l1khl{par}", t_kh)
        t_qh = b.dma("sync", qh[par][:].rearrange("p (r t) -> p r t", r=4),
                     gq_d.rearrange("(r c p) t -> p c r t", c=4, p=128)[:, lh, :, :],
                     wait=qh_free[par], sem=f"l1qhl{par}")
        qh_tok = (f"l1qhl{par}", t_qh)

        qq_mm = 0
        for s in range(8):
            b.op("scalar", lambda e, s=s, par=par: e.activation(
                out=qsq[:], in_=qh[par][:, s * 512:(s + 1) * 512], func=AF.Square),
                wait=[qh_tok, ("MM", qq_mm)] + (psF_free if s == 0 else []),
                inc="SC")
            t_qs = b.tok("SC")
            b.op("tensor", lambda e, s=s: e.matmul(
                ctx.psF[0:1, (s % 2) * 512:(s % 2) * 512 + 512], ctx.ones_bf[:],
                qsq[:], start=True, stop=True), inc="MM", wait=[("SC", t_qs)])
            qq_mm = b.tok("MM")
            b.op("vector", lambda e, s=s: e.reduce_max(
                out=mscr[:, 0:1], in_=ctx.psF[0:1, (s % 2) * 512:(s % 2) * 512 + 512],
                axis=AX.X), wait=[("MM", qq_mm)])
            if s == 0:
                b.op("vector", lambda e: e.tensor_copy(out=qm[:], in_=mscr[:, 0:1]),
                     inc="VE")
            else:
                b.op("vector", lambda e: e.tensor_max(qm[:], qm[:], mscr[:, 0:1]),
                     inc="VE")
            psF_free = [("VE", b.tok("VE"))]
            qq_mm = b.tok("MM")
        t_prev = 0
        t_prev_red = 0
        for s in range(8):
            b.op("scalar", lambda e, s=s, par=par: e.activation(
                out=ksq[:], in_=kh[par][:, s * 512:(s + 1) * 512], func=AF.Square),
                inc="SC",
                wait=[kh_tok, ("MM", t_prev), ("VE", t_prev_red)]
                + (psE_free if s == 0 else []))
            t_ks = b.tok("SC")
            b.op("tensor", lambda e: e.matmul(
                ctx.psE[0:1, 0:512], ctx.ones_bf[:], ksq[:],
                start=True, stop=True), inc="MM", wait=[("SC", t_ks)])
            t_prev = b.tok("MM")
            b.op("vector", lambda e: e.reduce_max(
                out=mscr[:, 1:2], in_=ctx.psE[0:1, 0:512], axis=AX.X),
                wait=[("MM", t_prev)])
            if s == 0:
                b.op("vector", lambda e: e.tensor_copy(out=kkm[:], in_=mscr[:, 1:2]),
                     inc="VE")
            else:
                b.op("vector", lambda e: e.tensor_max(kkm[:], kkm[:], mscr[:, 1:2]),
                     inc="VE")
            t_prev_red = b.tok("VE")
        psE_row_ve = b.tok("VE")
        b.op("vector", lambda e: e.tensor_mul(kkm[:], kkm[:], qm[:]), inc="VE")
        t_m1 = b.tok("VE")
        b.op("scalar", lambda e: e.activation(out=kkm[:], in_=kkm[:], func=AF.Sqrt),
             wait=[("VE", t_m1)])
        b.op("scalar", lambda e: e.mul(out=kkm[:], in_=kkm[:], mul=-L1_SCALE),
             inc="SC")
        t_m2 = b.tok("SC")
        b.op("tensor", lambda e: e.matmul(ctx.psD[:, 0:1], ctx.ones_f[:], kkm[:],
                                          start=True, stop=True),
             inc="MM", wait=[("SC", t_m2), psD_w])
        t_m3 = b.tok("MM")
        b.op("vector", lambda e: e.tensor_copy(out=mb[:], in_=ctx.psD[:, 0:1]),
             inc="VE", wait=[("MM", t_m3)])
        t_mb = b.tok("VE")
        psD_w = ("VE", t_mb)

        for qc in range(8):
            half = qc % 2
            qsl = slice(qc * 512, (qc + 1) * 512)
            esl = slice(half * 512, (half + 1) * 512)
            nkt = 4 * qc + 4
            for kt in range(nkt):
                si = kt % 3
                slab = ctx.slabs[si]
                ksl = slice(kt * 128, (kt + 1) * 128)
                b.op("tensor", lambda e, slab=slab, ksl=ksl, qsl=qsl, par=par: e.matmul(
                    slab[:], kh[par][:, ksl], qh[par][:, qsl],
                    start=True, stop=True),
                    inc="MM",
                    wait=list(ctx.slab_free[si]) + [qh_tok, kh_tok]
                    + ([("VE", psE_row_ve)] if kt == 0 and qc == 0 else []))
                t_sc_mm = b.tok("MM")
                diag = kt - 4 * qc
                if diag >= 0:
                    b.op("vector", lambda e, slab=slab, diag=diag: e.tensor_tensor(
                        slab[:], slab[:], maskt[diag][:], op=ALU.add),
                        inc="VE", wait=[("MM", t_sc_mm), mask_tok])
                    mask_ve = [("VE", b.tok("VE"))]
                else:
                    mask_ve = []
                b.op("scalar", lambda e, si=si, slab=slab: e.activation(
                    out=pr[si][:], in_=slab[:], func=AF.Exp, scale=L1_SCALE,
                    bias=mb[:]),
                    inc="SC",
                    wait=[("MM", t_sc_mm), ("VE", t_mb)] + mask_ve + pr_free[si])
                t_exp = b.tok("SC")
                ctx.slab_free[si] = [("SC", t_exp)]
                b.op("tensor", lambda e, si=si, esl=esl, kt=kt, nkt=nkt: e.matmul(
                    ctx.psF[0:1, esl], ctx.ones_bf[:], pr[si][:],
                    start=(kt == 0), stop=(kt == nkt - 1)),
                    wait=[("SC", t_exp)] + (psF_free if kt == 0 and qc < 2 else []))
                vhalf = kt // 16
                b.op("tensor", lambda e, si=si, esl=esl, kt=kt, nkt=nkt,
                     vhalf=vhalf, par=par: e.matmul(
                         ctx.psE[:, esl],
                         vp[vhalf][:, (kt % 16) * 256 + par * 128:
                                    (kt % 16) * 256 + par * 128 + 128],
                         pr[si][:], start=(kt == 0), stop=(kt == nkt - 1)),
                     inc="MM",
                     wait=([vp_tok[vhalf]] if kt % 16 == 0 else []))
                pr_free[si] = [("MM", b.tok("MM"))]
            sweep_mm = b.tok("MM")
            b.op("vector", lambda e, esl=esl: e.reciprocal(
                out=rs[:, 0:512], in_=ctx.psF[0:1, esl]),
                inc="VE", wait=[("MM", sweep_mm)])
            t_rs = b.tok("VE")
            psF_free = [("VE", t_rs)]
            b.op("tensor", lambda e: e.matmul(
                ctx.psD[:, 0:512], ctx.ones_f[:], rs[0:1, 0:512],
                start=True, stop=True),
                inc="MM", wait=[("VE", t_rs), psD_w])
            t_bc = b.tok("MM")
            b.op("vector", lambda e: e.tensor_copy(out=rb[:], in_=ctx.psD[:, 0:512]),
                 inc="VE", wait=[("MM", t_bc)])
            b.op("vector", lambda e, esl=esl: e.tensor_mul(
                ao16[:], ctx.psE[:, esl], rb[:]),
                inc="VE", wait=[("MM", sweep_mm), ("aowr", ao_st_tok[0])])
            t_n = b.tok("VE")
            psD_w = ("VE", t_n)
            psE_free = [("VE", t_n)]
            tq = qc // 2
            ao_st_tok[0] = b.dma(
                "sync", ao_d[tq * 512 + lh * 128: tq * 512 + lh * 128 + 128,
                             (qc % 2) * 512:(qc % 2) * 512 + 512],
                ao16[:], wait=[("VE", t_n)], sem="aowr")
        kh_free[par] = [("MM", b.tok("MM"))]
        qh_free[par] = [("MM", b.tok("MM"))]
        if par == 1:
            vp_free[0] = [("MM", b.tok("MM"))]
            vp_free[1] = [("MM", b.tok("MM"))]
    return ("aowr", ao_st_tok[0])



def emit_ffn1(b, ctx, *, gg_d, gu_d, gdd_d, ln):
    NHC = HF // 128
    h16 = [b.sb(f"f1h{o}", [128, 512], BF16) for o in range(NHC)]
    sg = [b.sb(f"f1sg{i}", [128, 512], BF16) for i in range(2)]
    h_tok = [("VE", 0)] * NHC
    sg_tok = [("SC", 0), ("SC", 0)]
    sg_used = [("VE", 0), ("VE", 0)]

    for th in range(2):
        tsl = slice(th * 512, (th + 1) * 512)
        rhs_x = [(ctx.x16[c][:, tsl], 128, ctx.x16_tok[c]) for c in range(DC)]

        def consume_g(meta, qt, ps, mm_tok):
            o = meta
            i = o % 2
            b.op("scalar", lambda e, ps=ps, i=i: e.activation(
                out=sg[i][:], in_=ps, func=AF.Silu),
                inc="SC", wait=[("MM", mm_tok), sg_used[i]])
            sg_tok[i] = ("SC", b.tok("SC"))
            return [sg_tok[i]]

        def consume_u(meta, qt, ps, mm_tok):
            o = meta
            i = o % 2
            b.op("vector", lambda e, o=o, ps=ps, i=i: e.tensor_mul(
                h16[o][:], sg[i][:], ps),
                inc="VE",
                wait=[("MM", mm_tok), sg_tok[i], h_tok[o]])
            sg_used[i] = ("VE", b.tok("VE"))
            h_tok[o] = ("VE", b.tok("VE"))
            return [h_tok[o]]

        for og in range(0, NHC, 2):
            sp = [(o * 128, 128, o) for o in (og, og + 1)]
            emit_wproj(b, ctx, tag="f1g", wsrc=gg_d, din_tot=D, w_f32=False,
                       dout_specs=sp, rhs=rhs_x, ntok=512, consume=consume_g)
            emit_wproj(b, ctx, tag="f1u", wsrc=gu_d, din_tot=D, w_f32=False,
                       dout_specs=sp, rhs=rhs_x, ntok=512, consume=consume_u)

        def consume_dn(meta, qt, ps, mm_tok, th=th):
            return ln.consume(meta, th, ps, mm_tok)
        emit_wproj(b, ctx, tag="f1d", wsrc=gdd_d, din_tot=HF, w_f32=False,
                   dout_specs=[(c * 128, 128, c) for c in range(DC)],
                   rhs=[(h16[o], 128, h_tok[o]) for o in range(NHC)],
                   ntok=512, consume=consume_dn)
        for o in range(NHC):
            h_tok[o] = ("MM", b.tok("MM"))
    ln.finish()


def build_p6_program():
    b = Builder("p6")
    ctx = Ctx(b)
    x1 = b.dram("x1", [D, TOK], F32, kind="ExternalInput")
    gup = b.dram("gup", [D, HF], BF16, kind="ExternalInput")
    gdn = b.dram("gdn", [HF, D], BF16, kind="ExternalInput")
    up_laP = b.dram("up_laP", [D, 128], F32, kind="ExternalInput")
    up_lb = b.dram("up_lb", [16, HF], F32, kind="ExternalInput")
    dn_laP = b.dram("dn_laP", [HF, 128], F32, kind="ExternalInput")
    dn_lb = b.dram("dn_lb", [16, D], F32, kind="ExternalInput")
    b_up_d = b.dram("b_up", [128, HF // 128], F32, kind="ExternalInput")
    b_dn_d = b.dram("b_dn", [128, DC], F32, kind="ExternalInput")
    x2 = b.dram("x2", [D, TOK], F32, kind="ExternalOutput")
    z_d = b.dram("z1", [D, TOK], F32)
    emit_load_x16(b, ctx, x1)
    b_dn = b.sb("b_dn_t", [128, DC], F32)
    t_bd = b.dma("sync", b_dn[:], b_dn_d, sem="bdnld")
    ln = LnRes(b, ctx, "ln1", x1, z_d, x2, bias=b_dn, bias_tok=("bdnld", t_bd))
    emit_ffn0(b, ctx, gup_d=gup, gdn_d=gdn, up_laP_d=up_laP, up_lb_d=up_lb,
              dn_laP_d=dn_laP, dn_lb_d=dn_lb, b_up_d=b_up_d, b_dn_d=b_dn_d, ln=ln)
    b.finish()
    return b


def build_p7_program():
    b = Builder("p7")
    ctx = Ctx(b)
    x2 = b.dram("x2", [D, TOK], F32, kind="ExternalInput")
    wq1 = b.dram("wq1", [D, D], F32, kind="ExternalInput")
    wk1 = b.dram("wk1", [D, D], F32, kind="ExternalInput")
    wv1 = b.dram("wv1", [D, D], F32, kind="ExternalInput")
    csc = b.dram("csc", [128, TOK], F32, kind="ExternalInput")
    css = b.dram("css", [128, TOK], F32, kind="ExternalInput")
    qst = b.dram("qst", [D, TOK], BF16, kind="ExternalOutput")
    kst = b.dram("kst1", [D, TOK], BF16, kind="ExternalOutput")
    vst = b.dram("vst1", [8 * TOK, 2 * DV], BF16, kind="ExternalOutput")
    emit_load_x16(b, ctx, x2)
    emit_p7_l1kv(b, ctx, wq1_d=wq1, wk1_d=wk1, wv1_d=wv1, csc_d=csc, css_d=css,
                 qst_d=qst, kst_d=kst, vst_d=vst)
    b.finish()
    return b


def build_p8_program():
    b = Builder("p8")
    ctx = Ctx(b)
    gq = b.dram("gq", [D, TOK], BF16, kind="ExternalInput")
    gk = b.dram("gk1", [D, TOK], BF16, kind="ExternalInput")
    gv = b.dram("gv1", [32 * TOK // 4, 2 * DV], BF16, kind="ExternalInput")
    ao = b.dram("ao", [D, TOK], BF16, kind="ExternalOutput")
    emit_p8_l1attn(b, ctx, gq_d=gq, gk_d=gk, gv_d=gv, ao_d=ao)
    b.finish()
    return b


def build_p9_program():
    b = Builder("p9")
    ctx = Ctx(b)
    aom = b.dram("aom", [D, TOK], BF16, kind="ExternalInput")
    x2 = b.dram("x2", [D, TOK], F32, kind="ExternalInput")
    wo1 = b.dram("wo1", [D, D], F32, kind="ExternalInput")
    gg = b.dram("gg", [D, HF], BF16, kind="ExternalInput")
    gu = b.dram("gu", [D, HF], BF16, kind="ExternalInput")
    gdd = b.dram("gdd", [HF, D], BF16, kind="ExternalInput")
    out = b.dram("out", [D, TOK], F32, kind="ExternalOutput")
    x3 = b.dram("x3", [D, TOK], F32)
    z2 = b.dram("z2", [D, TOK], F32)
    z3 = b.dram("z3", [D, TOK], F32)
    ao16 = [b.sb(f"aot{c}", [128, TOK], BF16) for c in range(DC)]
    t_ao = 0
    for c in range(DC):
        t_ao = b.dma("gpsimd", ao16[c][:], aom[c * 128:(c + 1) * 128, :], sem="aold")
    ln2 = LnRes(b, ctx, "ln2", x2, z2, x3)
    emit_wproj(b, ctx, tag="wo1", wsrc=wo1, din_tot=D, w_f32=True,
               dout_specs=[(c * 128, 128, c) for c in range(DC)],
               rhs=[(ao16[h], 128, ("aold", t_ao)) for h in range(NH)],
               ntok=TOK,
               consume=lambda meta, qt, ps, tok: ln2.consume(meta, qt, ps, tok))
    ln2.finish()
    ln3 = LnRes(b, ctx, "ln3", x3, z3, out, x_old_toks=ln2.xn_done)
    emit_ffn1(b, ctx, gg_d=gg, gu_d=gu, gdd_d=gdd, ln=ln3)
    b.finish()
    return b


def build_launch3_program():
    b = Builder("l3")
    ctx = Ctx(b)
    xT = b.dram("xT", [D, TOK], F32, kind="ExternalInput")
    wq = b.dram("wq", [D, NH * DQK], F32, kind="ExternalInput")
    wout = b.dram("wout", [D, D], F32, kind="ExternalInput")
    thT = b.dram("thT", [128, TOK], F32, kind="ExternalInput")
    gk = b.dram("gk", [4 * NH * DN, TOK], BF16, kind="ExternalInput")
    gkr = b.dram("gkr", [4 * DR, TOK], BF16, kind="ExternalInput")
    gv = b.dram("gv", [4 * 8 * TOK, 2 * DV], BF16, kind="ExternalInput")
    gup = b.dram("gup", [D, HF], BF16, kind="ExternalInput")
    gdn = b.dram("gdn", [HF, D], BF16, kind="ExternalInput")
    up_laP = b.dram("up_laP", [D, 128], F32, kind="ExternalInput")
    up_lb = b.dram("up_lb", [16, HF], F32, kind="ExternalInput")
    dn_laP = b.dram("dn_laP", [HF, 128], F32, kind="ExternalInput")
    dn_lb = b.dram("dn_lb", [16, D], F32, kind="ExternalInput")
    b_up_d = b.dram("b_up", [128, HF // 128], F32, kind="ExternalInput")
    b_dn_d = b.dram("b_dn", [128, DC], F32, kind="ExternalInput")
    wq1 = b.dram("wq1", [D, D], F32, kind="ExternalInput")
    wk1 = b.dram("wk1", [D, D], F32, kind="ExternalInput")
    wv1 = b.dram("wv1", [D, D], F32, kind="ExternalInput")
    csc = b.dram("csc", [128, TOK], F32, kind="ExternalInput")
    css = b.dram("css", [128, TOK], F32, kind="ExternalInput")
    x2 = b.dram("x2", [D, TOK], F32, kind="ExternalOutput")
    qst = b.dram("qst", [D, TOK], BF16, kind="ExternalOutput")
    kst1 = b.dram("kst1", [D, TOK], BF16, kind="ExternalOutput")
    vst1 = b.dram("vst1", [8 * TOK, 2 * DV], BF16, kind="ExternalOutput")
    x1_d = b.dram("x1s", [D, TOK], F32)
    qrp = b.dram("qrp", [NH * 64, TOK], BF16)
    z0 = b.dram("z0", [D, TOK], F32)
    z1 = b.dram("z1", [D, TOK], F32)

    emit_load_x16(b, ctx, xT)
    b.push_region()
    cs_c, cs_s, cs_tok = emit_theta_cs(b, ctx, thT, nfreq=128)
    attn16 = [b.sb(f"attn16_{h}", [128, TOK], BF16) for h in range(NH)]
    attn_toks = emit_p4_l0attn(b, ctx, wq_d=wq, qrp_d=qrp, gk_d=gk, gkr_d=gkr,
                               gv_d=gv, cs_c=cs_c, cs_s=cs_s, cs_tok=cs_tok,
                               attn16=attn16)
    ln0 = LnRes(b, ctx, "ln0", xT, z0, x1_d)
    emit_wproj(b, ctx, tag="wo", wsrc=wout, din_tot=D, w_f32=True,
               dout_specs=[(c * 128, 128, c) for c in range(DC)],
               rhs=[(attn16[h], 128, attn_toks[h][-1]) for h in range(NH)],
               ntok=TOK,
               consume=lambda meta, qt, ps, tok: ln0.consume(meta, qt, ps, tok))
    ln0.finish()
    b.barrier()
    b.pop_region()
    b.push_region()
    b_dn = b.sb("b_dn_t", [128, DC], F32)
    t_bd = b.dma("sync", b_dn[:], b_dn_d, sem="bdnld")
    ln1 = LnRes(b, ctx, "ln1", x1_d, z1, x2, bias=b_dn,
                bias_tok=("bdnld", t_bd), x_old_toks=ln0.xn_done)
    emit_ffn0(b, ctx, gup_d=gup, gdn_d=gdn, up_laP_d=up_laP, up_lb_d=up_lb,
              dn_laP_d=dn_laP, dn_lb_d=dn_lb, b_up_d=b_up_d, b_dn_d=b_dn_d,
              ln=ln1)
    b.barrier()
    b.pop_region()
    b.push_region()
    emit_p7_l1kv(b, ctx, wq1_d=wq1, wk1_d=wk1, wv1_d=wv1, csc_d=csc, css_d=css,
                 qst_d=qst, kst_d=kst1, vst_d=vst1)
    b.pop_region()
    b.finish()
    return b


_PROGRAMS = {}


def _get_prog(name, fn):
    if name not in _PROGRAMS:
        _PROGRAMS[name] = fn()
    return _PROGRAMS[name]


def kernel(**inputs):
    import ml_dtypes as mld
    sh, cores = prep_host(inputs)
    ncores = list(range(NCORE))

    b1, ins1, stg1 = _get_prog("p1", build_p1_program)
    in1 = []
    for c in ncores:
        m = {}
        for nm, (di_t, do_t, qk, sk, kind, zero) in WSPEC.items():
            m[nm + "_q"] = cores[c][qk]
            m[nm + "_s"] = cores[c][sk] if kind == "blk64" else sh[sk].astype(np.float32)
        in1.append(m)
    r1 = run_bass_kernel_spmd(b1.nc, in1, core_ids=ncores)
    gw = {}
    for nm in WSPEC:
        gw[nm] = np.ascontiguousarray(np.concatenate(
            [np.asarray(r1.results[c][nm + "_t"]).view(mld.bfloat16)
             for c in ncores], axis=0))

    b2 = _get_prog("p2", build_p2_program)
    in2 = [{"xT": cores[c]["xT"], "wkva": sh["wkva"].astype(np.float32),
            "wkvb": sh["wkvb"].astype(np.float32), "thT": cores[c]["thT"],
            "gamma": sh["gamma"].astype(np.float32),
            "beta": sh["beta"].astype(np.float32)} for c in ncores]
    r2 = run_bass_kernel_spmd(b2.nc, in2, core_ids=ncores)
    gk0, gkr0, gv0 = {}, {}, {}
    for grp in range(2):
        ks = np.concatenate([np.asarray(r2.results[grp * 4 + r]["kst"]).view(mld.bfloat16)
                             for r in range(4)], axis=0)
        krs = np.concatenate([np.asarray(r2.results[grp * 4 + r]["krst"]).view(mld.bfloat16)
                              for r in range(4)], axis=0)
        vs = np.concatenate([np.asarray(r2.results[grp * 4 + r]["vst"]).view(mld.bfloat16)
                             for r in range(4)], axis=0)
        for r in range(4):
            gk0[grp * 4 + r], gkr0[grp * 4 + r], gv0[grp * 4 + r] = ks, krs, vs

    b3 = _get_prog("l3", build_launch3_program)
    in3 = [{
        "xT": cores[c]["xT"], "wq": sh["wq"].astype(np.float32),
        "wout": sh["wout"].astype(np.float32), "thT": cores[c]["thT"],
        "gk": gk0[c], "gkr": gkr0[c], "gv": gv0[c],
        "gup": gw["up"], "gdn": gw["dn"],
        "up_laP": sh["up_laP"], "up_lb": sh["up_lb"].astype(np.float32),
        "dn_laP": sh["dn_laP"], "dn_lb": sh["dn_lb"].astype(np.float32),
        "b_up": sh["b_up"].astype(np.float32), "b_dn": sh["b_dn"].astype(np.float32),
        "wq1": sh["l1_wq"].astype(np.float32), "wk1": sh["l1_wk"].astype(np.float32),
        "wv1": sh["l1_wv"].astype(np.float32),
        "csc": cores[c]["l1_cosT"], "css": cores[c]["l1_sinT"],
    } for c in ncores]
    r3 = run_bass_kernel_spmd(b3.nc, in3, core_ids=ncores)

    def a2a(res, key, rows):
        outs = {}
        for grp in range(2):
            stages = [np.asarray(res.results[grp * 4 + r][key]).view(mld.bfloat16)
                      for r in range(4)]
            sh_ = rows // 4
            for r in range(4):
                outs[grp * 4 + r] = np.concatenate(
                    [stages[s_][r * sh_:(r + 1) * sh_] for s_ in range(4)], axis=0)
        return outs

    gq1 = a2a(r3, "qst", D)
    gk1 = a2a(r3, "kst1", D)
    gv1 = a2a(r3, "vst1", 8 * TOK)

    b4 = _get_prog("p8", build_p8_program)
    in4 = [{"gq": gq1[c], "gk1": gk1[c], "gv1": gv1[c]} for c in ncores]
    r4 = run_bass_kernel_spmd(b4.nc, in4, core_ids=ncores)
    aom = a2a(r4, "ao", D)

    b5 = _get_prog("p9", build_p9_program)
    in5 = [{"aom": aom[c],
            "x2": np.ascontiguousarray(np.asarray(r3.results[c]["x2"])),
            "wo1": sh["l1_wo"].astype(np.float32),
            "gg": gw["g"], "gu": gw["u"], "gdd": gw["dd"]} for c in ncores]
    r5 = run_bass_kernel_spmd(b5.nc, in5, core_ids=ncores)

    out = np.zeros((B, S, D), np.float32)
    for c in ncores:
        bidx, j = c // 4, c % 4
        out[bidx, j * TOK:(j + 1) * TOK, :] = np.asarray(r5.results[c]["out"]).T
    return out
